# version 05
# baseline (speedup 1.0000x reference)
import numpy as np
from contextlib import ExitStack

import concourse.bass as bass
import concourse.mybir as mybir
from concourse.bass_utils import run_bass_kernel_spmd

F32 = mybir.dt.float32
BF16 = mybir.dt.bfloat16
AF = mybir.ActivationFunctionType
OP = mybir.AluOpType
AX = mybir.AxisListType

P = 128
D = 1024
S = 4096
SQ = 2048
HD = 4096
NT = 32
NG = 8
NCH = 16
NC3 = 8
EPS = 1e-5
CEXP = 0.6 - 20.0
KB = 1024


class Sem:
    def __init__(self, h):
        self.h = h
        self.n = 0


class Prog:
    ENGS = ("sync", "gpsimd", "vector", "scalar", "tensor")

    def __init__(self):
        self.ops = {e: [] for e in self.ENGS}

    def em(self, e, f, sem=None, by=1):
        if sem is None:
            self.ops[e].append(f)
            return None
        sem.n += by
        self.ops[e].append(lambda E, f=f, h=sem.h, b=by: f(E).then_inc(h, b))
        return sem.n

    def w(self, e, sem, val):
        if val is None or val <= 0:
            return
        self.ops[e].append(lambda E, h=sem.h, v=val: E.wait_ge(h, v))

    def dma(self, e, sem, out, in_):
        self.w(e, sem, sem.n)
        return self.em(e, lambda E, o=out, i=in_: E.dma_start(out=o, in_=i),
                       sem, 16)


def build_nc(dbg=(), phases=3, sub=""):
    nc = bass.Bass("TRN2", target_bir_lowering=False, debug=False)

    xb_d = nc.dram_tensor("xb", [S, D], F32, kind="ExternalInput")
    Wq_d = nc.dram_tensor("Wq", [D, D], F32, kind="ExternalInput")
    Wk_d = nc.dram_tensor("Wk", [D, D], F32, kind="ExternalInput")
    Wv_d = nc.dram_tensor("Wv", [D, D], F32, kind="ExternalInput")
    Wo_d = nc.dram_tensor("Wo", [D, D], F32, kind="ExternalInput")
    bq_d = nc.dram_tensor("bq", [D], F32, kind="ExternalInput")
    bk_d = nc.dram_tensor("bk", [D], F32, kind="ExternalInput")
    bv_d = nc.dram_tensor("bv", [D], F32, kind="ExternalInput")
    bo_d = nc.dram_tensor("bo", [D], F32, kind="ExternalInput")
    w1_d = nc.dram_tensor("w1", [D, HD], F32, kind="ExternalInput")
    b1_d = nc.dram_tensor("b1", [HD], F32, kind="ExternalInput")
    w2_d = nc.dram_tensor("w2", [HD, D], F32, kind="ExternalInput")
    b2_d = nc.dram_tensor("b2", [D], F32, kind="ExternalInput")
    out_d = nc.dram_tensor("out", [SQ, D], F32, kind="ExternalOutput")
    dbg_names = [n for n, _ in dbg]
    x1s_kind = "ExternalOutput" if "x1s" in dbg_names else "Internal"
    x1s_d = nc.dram_tensor("x1s", [SQ, D], F32, kind=x1s_kind)
    w1b_d = nc.dram_tensor("w1b", [D, HD], BF16, kind="Internal")
    w2b_d = nc.dram_tensor("w2b", [HD, D], BF16, kind="Internal")
    dbg_outs = {}
    for name, shape in dbg:
        if name == "x1s":
            continue
        dbg_outs[name] = nc.dram_tensor("dbg_" + name, shape, F32,
                                        kind="ExternalOutput")

    AR = 192512
    arena = nc.alloc_sbuf_tensor("arena", [P, AR], mybir.dt.uint8)
    base = nc.lookup_mloc(arena).addr

    def at(name, shape, dtype, off):
        return nc.alloc_sbuf_tensor_at(name, shape, dtype, offset=base + off)

    A, B, C, Dr, E, F = 0, 65536, 131072, 147456, 163840, 176128
    ln1_tm = at("ln1_tm", [P, NT * KB], BF16, A)
    ln1_fm = at("ln1_fm", [P, 8 * S], BF16, B)
    M_sb = at("M_sb", [P, 8 * KB], BF16, C)
    G_sb = at("G_sb", [P, 8 * KB], BF16, Dr)
    xstg = [at(f"xstg{i}", [P, KB], F32, E + i * 4096) for i in range(3)]
    qp_sb = at("qp_sb", [P, 8 * 256], BF16, E)
    zf_sb = at("zf_sb", [P, 8 * 256], BF16, E + 4096)
    ptb = at("ptb", [P, 2 * 512], BF16, E + 8192)
    bo_row = at("bo_row", [1, D], BF16, E + 8192)
    bias_st = at("bias_st", [32, P], F32, E + 10240)
    wot_sb = at("wot_sb", [P, 8 * KB], BF16, F)
    xres = at("xres", [P, 2 * KB], F32, F)
    x1b = at("x1b", [P, 2 * KB], F32, F + 8192)
    wA = at("wA", [P, 8 * KB], BF16, A + 32 * KB)
    wB = at("wB", [P, 8 * KB], BF16, A + 48 * KB)
    w1_sb = at("w1_sb", [P, 8 * HD], BF16, A)
    w2_sb = at("w2_sb", [P, NT * KB], BF16, B)
    x1stg = [at(f"x1stg{i}", [P, 2 * KB], F32, C + i * 8192) for i in range(2)]
    outstg = [at(f"outstg{i}", [P, 2 * KB], F32, Dr + i * 8192) for i in range(2)]
    h_sb = at("h_sb", [P, 32 * 256], BF16, E)
    ln2_fm = [at(f"ln2fm{i}", [P, 8 * 256], BF16, F + 4096 + i * 4096)
              for i in range(2)]
    ln2tmp = at("ln2tmp", [P, KB], BF16, F + 12288)
    b2_row = at("b2_row", [1, D], BF16, F + 14336)

    def sb(name, shape, dtype):
        return nc.alloc_sbuf_tensor(name, shape, dtype)

    ident_bf = sb("ident_bf", [P, P], BF16)
    ident_f = sb("ident_f", [32, 32], F32)
    ones_bf = sb("ones_bf", [1, P], BF16)
    ones_col = sb("ones_col", [P, 1], BF16)
    bqkv = sb("bqkv", [P, 24], BF16)
    wuv = sb("wuv", [P, 16], BF16)
    cq_f = sb("cq_f", [1, 1], F32)
    cq_row = sb("cq_row", [1, P], BF16)
    b1_col = sb("b1_col", [P, 32], F32)
    bpp_row = sb("bpp_row", [1, D], BF16)
    u_row = sb("u_row", [1, SQ], BF16)
    vcol = sb("vcol", [P, NT], F32)
    vch = sb("vch", [P, NT], F32)
    abuf = sb("abuf", [P, 512], F32)
    bbuf = sb("bbuf", [P, 512], F32)
    dbuf = sb("dbuf", [P, 512], F32)
    l_bf = sb("l_bf", [1, 256], BF16)
    rl = sb("rl", [P, 2], F32)
    consts = sb("consts", [P, 2], F32)
    x1t = sb("x1t", [P, KB], F32)
    bnst = sb("bnst", [P, 4 * 12], F32)
    bnagg = sb("bnagg", [P, 4 * 2], F32)
    sqv = sb("sqv", [P, 4], F32)
    rstdc = sb("rstdc", [P, 4], F32)
    negmr = sb("negmr", [P, 4], F32)
    mrstd = sb("mrstd", [P, 2], F32)

    pqk = nc.alloc_psum_tensor("pqk", [P, 1024], F32)
    pz = nc.alloc_psum_tensor("pz", [P, 2048], F32)
    po = nc.alloc_psum_tensor("po", [P, 512], F32)
    pbf = nc.alloc_psum_tensor("pbf", [P, 1024], BF16)

    p = Prog()

    with ExitStack() as ctx:
        sem = lambda n: Sem(ctx.enter_context(nc.semaphore(n)))
        d_wg = sem("d_wg")
        d_x = sem("d_x")
        d_x1w = sem("d_x1w")
        d_x1r = sem("d_x1r")
        d_out = sem("d_out")
        d_w2a = sem("d_w2a")
        d_wsc = sem("d_wsc")
        d_cast = sem("d_cast")
        d_dbg = sem("d_dbg")
        pe = sem("pe")
        vv = sem("vv")
        ss = sem("ss")
        gg = sem("gg")

        NSLOT = 4
        lnstate = {"i": 0, "sqrt": [0] * NSLOT, "recip": [0] * NSLOT,
                   "aggr": [0] * NSLOT, "mread": [0] * NSLOT,
                   "mread_ss": [0] * NSLOT}

        def emit_stats(src_fn, width, pre_wait=None):
            i = lnstate["i"]
            lnstate["i"] += 1
            slot = i % NSLOT
            for eng, sm, val in (pre_wait or []):
                p.w(eng, sm, val)
            p.w("vector", ss, lnstate["sqrt"][slot])
            p.w("vector", vv, lnstate["aggr"][slot])
            n5 = width // 512
            for j in range(n5):
                v_bn = p.em("vector", lambda E, j=j, slot=slot: E.bn_stats(
                    bnst[:, slot * 12 + j * 6: slot * 12 + (j + 1) * 6],
                    src_fn(E, j * 512, 512)), vv)
            p.w("vector", vv, v_bn)
            p.w("vector", vv, lnstate["mread"][slot])
            v_aggr = p.em("vector", lambda E, slot=slot, n5=n5: E.bn_aggr(
                bnagg[:, slot * 2: slot * 2 + 2],
                bnst[:, slot * 12: slot * 12 + 6 * n5]), vv)
            lnstate["aggr"][slot] = v_aggr
            p.w("scalar", vv, v_aggr)
            p.w("scalar", vv, lnstate["recip"][slot])
            s_sq = p.em("scalar", lambda E, slot=slot: E.activation(
                sqv[:, slot:slot + 1], bnagg[:, slot * 2 + 1: slot * 2 + 2],
                AF.Sqrt, bias=consts[:, 1:2], scale=1.0), ss)
            lnstate["sqrt"][slot] = s_sq
            p.w("vector", ss, s_sq)
            p.w("vector", ss, lnstate["mread_ss"][slot])
            v_rec = p.em("vector", lambda E, slot=slot: E.reciprocal(
                rstdc[:, slot:slot + 1], sqv[:, slot:slot + 1]), vv)
            lnstate["recip"][slot] = v_rec
            return slot, v_rec

        def mean_ap(slot):
            return bnagg[:, slot * 2: slot * 2 + 1]

        def mark_mean_read(slot, count):
            lnstate["mread"][slot] = max(lnstate["mread"][slot], count)

        def mark_read_ss(slot, count):
            lnstate["mread_ss"][slot] = max(lnstate["mread_ss"][slot], count)

        p.em("gpsimd", lambda E: E.memset(ones_bf[:], 1.0), gg)
        p.em("gpsimd", lambda E: E.memset(ones_col[:], 1.0), gg)
        p.em("gpsimd", lambda E: E.memset(consts[:, 0:1], CEXP), gg)
        p.em("gpsimd", lambda E: E.memset(consts[:, 1:2], EPS), gg)
        p.em("gpsimd", lambda E: E.memset(ident_bf[:], 0.0), gg)
        p.em("gpsimd", lambda E: E.memset(ident_f[:], 0.0), gg)
        p.w("gpsimd", gg, 6)
        p.em("gpsimd", lambda E: E.affine_select(
            out=ident_bf[:], in_=ident_bf[:], compare_op=OP.not_equal,
            fill=1.0, base=0, pattern=[[-1, P]], channel_multiplier=1), gg)
        g_setup = p.em("gpsimd", lambda E: E.affine_select(
            out=ident_f[:], in_=ident_f[:], compare_op=OP.not_equal,
            fill=1.0, base=0, pattern=[[-1, 32]], channel_multiplier=1), gg)

        wq_ap = Wq_d[:, :].rearrange("(c p) d -> p c d", p=P)
        wk_ap = Wk_d[:, :].rearrange("(c p) d -> p c d", p=P)
        wv_ap = Wv_d[:, :].rearrange("(c p) d -> p c d", p=P)
        wo_ap = Wo_d[:, :].rearrange("(c p) d -> p c d", p=P)
        wA3 = wA[:].rearrange("p (c d) -> p c d", c=8)
        wB3 = wB[:].rearrange("p (c d) -> p c d", c=8)
        d_wq = p.dma("gpsimd", d_wg, wA3, wq_ap)
        d_wk = p.dma("gpsimd", d_wg, wB3, wk_ap)
        d_bq = p.dma("gpsimd", d_wg, bias_st[0:8, :],
                     bq_d[:].rearrange("(c p) -> c p", p=P))
        d_bk = p.dma("gpsimd", d_wg, bias_st[8:16, :],
                     bk_d[:].rearrange("(c p) -> c p", p=P))
        d_bv = p.dma("gpsimd", d_wg, bias_st[16:24, :],
                     bv_d[:].rearrange("(c p) -> c p", p=P))
        d_bo = p.dma("gpsimd", d_wg, bo_row[0:1, :],
                     bo_d[:].rearrange("(a d) -> a d", a=1))

        p.w("scalar", gg, g_setup)
        p.w("tensor", gg, g_setup)
        p.w("tensor", d_wg, d_bv)
        t_bias = p.em("tensor", lambda E: E.transpose(
            po[:, 0:24], bias_st[0:24, :], ident_f[0:24, 0:24]), pe)
        p.w("vector", pe, t_bias)
        v_bqkv = p.em("vector", lambda E: E.tensor_copy(bqkv[:], po[:, 0:24]), vv)
        if sub == "bias":
            return finish(nc, p)

        p.w("tensor", d_wg, d_wk)
        m_evac = {}
        for dt in range(8):
            for nh in range(2):
                ti = dt * 2 + nh
                sl = ti % 4
                if ti >= 4:
                    p.w("tensor", vv, m_evac[ti - 4])
                for ec in range(8):
                    fn = lambda E, dt=dt, nh=nh, ec=ec, sl=sl: E.matmul(
                        pz[:, sl * 512:(sl + 1) * 512],
                        wA[:, ec * KB + dt * P: ec * KB + (dt + 1) * P],
                        wB[:, ec * KB + nh * 512: ec * KB + (nh + 1) * 512],
                        start=(ec == 0), stop=(ec == 7))
                    t_m = p.em("tensor", fn, pe) if ec == 7 else p.em("tensor", fn)
                p.w("vector", pe, t_m)
                m_evac[ti] = p.em("vector", lambda E, dt=dt, nh=nh, sl=sl:
                    E.tensor_copy(
                        M_sb[:, dt * KB + nh * 512: dt * KB + (nh + 1) * 512],
                        pz[:, sl * 512:(sl + 1) * 512]), vv)
        v_m_done = vv.n
        if sub == "m":
            return finish(nc, p)

        p.w("tensor", vv, v_bqkv)
        for dt in range(8):
            for ec in range(8):
                p.em("tensor", lambda E, dt=dt, ec=ec: E.matmul(
                    po[:, 64 + dt:65 + dt],
                    wA[:, ec * KB + dt * P: ec * KB + (dt + 1) * P],
                    bqkv[:, 8 + ec:9 + ec], start=(ec == 0), stop=(ec == 7)))
        for dt in range(8):
            for ec in range(8):
                p.em("tensor", lambda E, dt=dt, ec=ec: E.matmul(
                    po[:, 72 + dt:73 + dt],
                    wB[:, ec * KB + dt * P: ec * KB + (dt + 1) * P],
                    bqkv[:, ec:ec + 1], start=(ec == 0), stop=(ec == 7)))
        for ec in range(8):
            fn = lambda E, ec=ec: E.matmul(
                po[0:1, 80:81], bqkv[:, ec:ec + 1], bqkv[:, 8 + ec:9 + ec],
                start=(ec == 0), stop=(ec == 7))
            t_wuv = p.em("tensor", fn, pe) if ec == 7 else p.em("tensor", fn)
        p.w("vector", pe, t_wuv)
        p.em("vector", lambda E: E.tensor_copy(wuv[:], po[:, 64:80]), vv)
        v_cqf = p.em("vector", lambda E: E.tensor_copy(cq_f[0:1, :],
                                                       po[0:1, 80:81]), vv)
        p.w("vector", vv, v_cqf)
        v_cqr = p.em("vector", lambda E: E.tensor_scalar(
            cq_row[0:1, :], ones_bf[0:1, :], cq_f[0:1, 0:1], None, OP.mult), vv)
        if sub == "wuv":
            return finish(nc, p)

        p.w("gpsimd", pe, t_wuv)
        d_wv = p.dma("gpsimd", d_wg, wA3, wv_ap)
        d_wo = p.dma("gpsimd", d_wg, wB3, wo_ap)
        d_b1 = p.dma("gpsimd", d_wg, bias_st[0:32, :],
                     b1_d[:].rearrange("(c p) -> c p", p=P))
        p.dma("gpsimd", d_cast, w1b_d[:, :], w1_d[:, :])
        cast_done = p.dma("gpsimd", d_cast, w2b_d[:, :], w2_d[:, :])
        p.w("tensor", d_wg, d_b1)
        p.w("tensor", vv, v_bqkv)
        p.w("tensor", vv, v_cqf)
        t_b1 = p.em("tensor", lambda E: E.transpose(
            po[:, 0:32], bias_st[0:32, :], ident_f[0:32, 0:32]), pe)
        p.w("vector", pe, t_b1)
        v_b1 = p.em("vector", lambda E: E.tensor_copy(b1_col[:], po[:, 0:32]), vv)
        if sub == "b1":
            return finish(nc, p)

        p.w("tensor", d_wg, d_wo)
        wot_ev = {}
        for ec in range(8):
            if ec >= 1:
                p.w("tensor", vv, wot_ev[ec - 1])
            for j in range(8):
                fn = lambda E, ec=ec, j=j: E.transpose(
                    pbf[:, j * P:(j + 1) * P],
                    wB[:, j * KB + ec * P: j * KB + (ec + 1) * P],
                    ident_bf[:])
                t_wt = p.em("tensor", fn, pe) if j == 7 else p.em("tensor", fn)
            p.w("vector", pe, t_wt)
            wot_ev[ec] = p.em("vector", lambda E, ec=ec: E.tensor_copy(
                wot_sb[:, ec * KB:(ec + 1) * KB], pbf[:]), vv)
        v_wot_done = vv.n
        if sub == "wot":
            return finish(nc, p)

        p.w("tensor", d_wg, d_wv)
        p.w("tensor", vv, v_wot_done)
        g_evac = {}
        for dt in range(8):
            for nh in range(2):
                ti = dt * 2 + nh
                sl = ti % 4
                if ti >= 4:
                    p.w("tensor", vv, g_evac[ti - 4])
                for ec in range(8):
                    fn = lambda E, dt=dt, nh=nh, ec=ec, sl=sl: E.matmul(
                        pz[:, sl * 512:(sl + 1) * 512],
                        wA[:, ec * KB + dt * P: ec * KB + (dt + 1) * P],
                        wot_sb[:, ec * KB + nh * 512: ec * KB + (nh + 1) * 512],
                        start=(ec == 0), stop=(ec == 7))
                    t_g = p.em("tensor", fn, pe) if ec == 7 else p.em("tensor", fn)
                p.w("vector", pe, t_g)
                g_evac[ti] = p.em("vector", lambda E, dt=dt, nh=nh, sl=sl:
                    E.tensor_copy(
                        G_sb[:, dt * KB + nh * 512: dt * KB + (nh + 1) * 512],
                        pz[:, sl * 512:(sl + 1) * 512]), vv)

        p.w("tensor", d_wg, d_bo)
        v_bpp = v_b1
        for eh in range(2):
            p.w("tensor", vv, v_bpp)
            for ec in range(8):
                p.em("tensor", lambda E, eh=eh, ec=ec: E.matmul(
                    po[0:1, 0:512], bqkv[:, 16 + ec:17 + ec],
                    wot_sb[:, ec * KB + eh * 512: ec * KB + (eh + 1) * 512],
                    start=(ec == 0), stop=False))
            t_bpp = p.em("tensor", lambda E, eh=eh: E.matmul(
                po[0:1, 0:512], ones_bf[0:1, 0:1],
                bo_row[0:1, eh * 512:(eh + 1) * 512],
                start=False, stop=True), pe)
            p.w("vector", pe, t_bpp)
            v_bpp = p.em("vector", lambda E, eh=eh: E.tensor_copy(
                bpp_row[0:1, eh * 512:(eh + 1) * 512], po[0:1, 0:512]), vv)
        t_p0_done = pe.n
        v_p0_done = vv.n
        if sub == "p0a":
            return finish(nc, p)

        apply_v = {}
        fm_ev = {}
        for T in range(NT):
            par = T % 3
            if T == 2:
                p.w("sync", pe, t_p0_done)
            if T >= 3:
                p.w("sync", vv, apply_v[T - 3])
            dx = p.dma("sync", d_x, xstg[par][:], xb_d[T * P:(T + 1) * P, :])
            slot, v_rec = emit_stats(
                lambda E, lo, n, par=par: xstg[par][:, lo:lo + n], KB,
                pre_wait=[("vector", d_x, dx)])
            p.w("vector", vv, v_rec)
            if T == 16:
                p.w("vector", pe, t_p0_done)
            apply_v[T] = p.em("vector", lambda E, T=T, par=par, slot=slot:
                E.tensor_scalar(
                    ln1_tm[:, T * KB:(T + 1) * KB], xstg[par][:],
                    mean_ap(slot), rstdc[:, slot:slot + 1],
                    OP.subtract, OP.mult), vv)
            mark_mean_read(slot, apply_v[T])
            p.w("tensor", vv, apply_v[T])
            if T >= 1:
                p.w("tensor", vv, fm_ev[T - 1])
            else:
                p.w("tensor", vv, v_p0_done)
            for j in range(8):
                fn = lambda E, T=T, j=j: E.transpose(
                    pbf[:, j * P:(j + 1) * P],
                    ln1_tm[:, T * KB + j * P: T * KB + (j + 1) * P],
                    ident_bf[:])
                t_tr = p.em("tensor", fn, pe) if j == 7 else p.em("tensor", fn)
            p.w("vector", pe, t_tr)
            fm_ev[T] = p.em("vector", lambda E, T=T: E.tensor_copy(
                ln1_fm[:].rearrange("p (i k) -> p i k", i=8)
                [:, :, T * P:(T + 1) * P],
                pbf[:].rearrange("p (i t) -> p i t", i=8)), vv)
        v_p1_done = vv.n
        if sub == "p1":
            return finish(nc, p)

        u_ev = {}
        for cu in range(8):
            p.w("tensor", vv, fm_ev[2 * cu + 1])
            p.w("tensor", vv, v_bpp if cu == 0 else u_ev[cu - 1])
            for dc in range(8):
                fn = lambda E, cu=cu, dc=dc: E.matmul(
                    po[0:1, 0:256], wuv[:, dc:dc + 1],
                    ln1_fm[:, dc * S + cu * 256: dc * S + (cu + 1) * 256],
                    start=(dc == 0), stop=(dc == 7))
                t_u = p.em("tensor", fn, pe) if dc == 7 else p.em("tensor", fn)
            p.w("vector", pe, t_u)
            u_ev[cu] = p.em("vector", lambda E, cu=cu: E.tensor_copy(
                u_row[0:1, cu * 256:(cu + 1) * 256], po[0:1, 0:256]), vv)
        p.w("tensor", vv, v_p1_done)
        p.w("tensor", vv, v_cqr)
        for T in range(NT):
            for dc in range(8):
                p.em("tensor", lambda E, T=T, dc=dc: E.matmul(
                    pqk[:, T:T + 1],
                    ln1_fm[:, dc * S + T * P: dc * S + (T + 1) * P],
                    wuv[:, 8 + dc:9 + dc], start=(dc == 0), stop=False))
            fn = lambda E, T=T: E.matmul(
                pqk[:, T:T + 1], cq_row[0:1, 0:P], ones_bf[0:1, 0:1],
                start=False, stop=True)
            t_vc = p.em("tensor", fn, pe) if T == NT - 1 else p.em("tensor", fn)
        p.w("vector", pe, t_vc)
        p.em("vector", lambda E: E.tensor_copy(vcol[:], pqk[:, 0:NT]))
        v_vcol = p.em("vector", lambda E: E.tensor_scalar(
            vch[:], pqk[:, 0:NT], 0.5, None, OP.mult), vv)

        def emit_dbg_taps(tapmap):
            for name, dout in dbg_outs.items():
                t, sm, val = tapmap[name]
                p.w("gpsimd", sm, val)
                p.em("gpsimd", lambda E, dout=dout, t=t:
                     E.dma_start(out=dout[:, :], in_=t[:]), d_dbg, 16)
            if dbg_outs:
                p.w("gpsimd", d_dbg, 16 * len(dbg_outs))

        def tapmap():
            return {
                "ln1_tm": (ln1_tm, vv, v_p1_done),
                "ln1_fm": (ln1_fm, vv, v_p1_done),
                "M": (M_sb, vv, v_m_done),
                "G": (G_sb, vv, v_p0_done),
                "wot": (wot_sb, vv, v_wot_done),
                "u": (u_row, vv, v_vcol),
                "vcol": (vcol, vv, v_vcol),
                "bpp": (bpp_row, vv, v_p0_done),
                "b1col": (b1_col, vv, v_p0_done),
                "qp": (qp_sb, vv, vv.n),
                "zf": (zf_sb, vv, vv.n),
                "x1b": (x1b, vv, vv.n),
                "lbf": (l_bf, vv, vv.n),
                "rl": (rl, vv, vv.n),
                "w1sb": (w1_sb, vv, vv.n),
                "w2sb": (w2_sb, vv, vv.n),
                "hsb": (h_sb, vv, vv.n),
                "b2row": (b2_row, vv, vv.n),
            }

        if phases < 2:
            emit_dbg_taps(tapmap())
            return finish(nc, p)

        s_done = {}
        relu_done = {}
        bb_done = {}
        d1_done = {}
        e_done = {}
        exp_done = {}
        z_done = {}
        qp_ev = {}
        o_ev = {}
        recip_v = {0: v_vcol}
        xr_d = {}
        x1_last = {}

        def emit_qprime(g):
            if g == 0:
                p.w("tensor", vv, v_p1_done)
                p.w("tensor", vv, v_m_done)
            for dt in range(8):
                for ec in range(8):
                    fn = lambda E, g=g, dt=dt, ec=ec: E.matmul(
                        pz[:, dt * 256:(dt + 1) * 256],
                        M_sb[:, ec * KB + dt * P: ec * KB + (dt + 1) * P],
                        ln1_fm[:, ec * S + g * 256: ec * S + (g + 1) * 256],
                        start=(ec == 0), stop=(ec == 7))
                    last = (dt == 7 and ec == 7)
                    t_qp = p.em("tensor", fn, pe) if last else p.em("tensor", fn)
            p.w("vector", pe, t_qp)
            qp_ev[g] = p.em("vector", lambda E: E.tensor_copy(qp_sb[:], pz[:]), vv)

        def emit_scores(g, ci):
            par = ci % 2
            gi = g * NCH + ci
            if gi >= 2:
                p.w("tensor", vv, relu_done[gi - 2])
                p.w("tensor", ss, bb_done[gi - 2])
            if ci <= 1 and g >= 1:
                p.w("tensor", vv, o_ev[(g - 1, 2 + ci)])
            if ci == 0:
                p.w("tensor", vv, qp_ev[g])
            for ki in range(2):
                kc = 2 * ci + ki
                for dc in range(8):
                    p.em("tensor", lambda E, par=par, ki=ki, kc=kc, dc=dc:
                        E.matmul(
                            pqk[:, par * 512 + ki * 256:
                                par * 512 + (ki + 1) * 256],
                            ln1_fm[:, dc * S + kc * P: dc * S + (kc + 1) * P],
                            qp_sb[:, dc * 256:(dc + 1) * 256],
                            start=(dc == 0), stop=False))
                fn = lambda E, par=par, ki=ki, g=g: E.matmul(
                    pqk[:, par * 512 + ki * 256: par * 512 + (ki + 1) * 256],
                    ones_bf[0:1, 0:P], u_row[0:1, g * 256:(g + 1) * 256],
                    start=False, stop=True)
                t_s = p.em("tensor", fn, pe) if ki == 1 else p.em("tensor", fn)
            s_done[gi] = t_s

        def emit_theta(g, ci):
            par = ci % 2
            gi = g * NCH + ci
            p.w("scalar", pe, s_done[gi])
            if gi >= 1:
                p.w("scalar", vv, e_done[gi - 1])
                p.w("scalar", ss, exp_done[gi - 1])
            for ki in range(2):
                kc = 2 * ci + ki
                p.em("scalar", lambda E, par=par, ki=ki, kc=kc: E.activation(
                    abuf[:, ki * 256:(ki + 1) * 256],
                    pqk[:, par * 512 + ki * 256: par * 512 + (ki + 1) * 256],
                    AF.Tanh, bias=vch[:, kc:kc + 1], scale=0.5))
            for ki in range(2):
                kc = 2 * ci + ki
                fn = lambda E, par=par, ki=ki, kc=kc: E.activation(
                    bbuf[:, ki * 256:(ki + 1) * 256],
                    pqk[:, par * 512 + ki * 256: par * 512 + (ki + 1) * 256],
                    AF.Tanh, bias=vcol[:, kc:kc + 1], scale=1.0)
                t_bb = p.em("scalar", fn, ss) if ki == 1 else p.em("scalar", fn)
            bb_done[gi] = t_bb
            p.w("vector", ss, bb_done[gi])
            d1_done[gi] = p.em("vector", lambda E: E.scalar_tensor_tensor(
                dbuf[:], bbuf[:], 1.5, abuf[:], OP.mult, OP.add), vv)
            p.w("vector", vv, d1_done[gi])
            for ki in range(2):
                kc = 2 * ci + ki
                fn = lambda E, par=par, ki=ki, kc=kc: E.tensor_scalar(
                    abuf[:, ki * 256:(ki + 1) * 256],
                    pqk[:, par * 512 + ki * 256: par * 512 + (ki + 1) * 256],
                    vcol[:, kc:kc + 1], 0.0, OP.add, OP.max)
                t_rl = p.em("vector", fn, vv) if ki == 1 else p.em("vector", fn)
            relu_done[gi] = t_rl
            p.w("vector", vv, relu_done[gi])
            if gi >= 1:
                p.w("vector", ss, exp_done[gi - 1])
            e_done[gi] = p.em("vector", lambda E: E.tensor_tensor(
                bbuf[:], abuf[:], dbuf[:], OP.add), vv)
            p.w("scalar", vv, e_done[gi])
            if gi >= 2:
                p.w("scalar", pe, z_done[gi - 2])
            exp_done[gi] = p.em("scalar", lambda E, par=par: E.activation(
                ptb[:, par * 512:(par + 1) * 512], bbuf[:], AF.Exp,
                bias=consts[:, 0:1], scale=0.1), ss)

        def emit_z(g, ci):
            par = ci % 2
            gi = g * NCH + ci
            p.w("tensor", ss, exp_done[gi])
            if ci == 0 and g >= 1:
                p.w("tensor", vv, o_ev[(g - 1, 3)])
            for ki in range(2):
                kc = 2 * ci + ki
                for dt in range(8):
                    p.em("tensor", lambda E, par=par, ki=ki, kc=kc, dt=dt:
                        E.matmul(
                            pz[:, dt * 256:(dt + 1) * 256],
                            ln1_tm[:, kc * KB + dt * P: kc * KB + (dt + 1) * P],
                            ptb[:, par * 512 + ki * 256:
                                par * 512 + (ki + 1) * 256],
                            start=(ci == 0 and ki == 0 and dt % 2 == 0),
                            stop=(ci == NCH - 1 and ki == 1 and dt % 2 == 1)))
            z_done[gi] = p.em("tensor", lambda E, par=par, ci=ci: E.matmul(
                po[0:1, 0:512], ones_col[:, 0:1],
                ptb[:, par * 512: (par + 1) * 512],
                start=(ci == 0), stop=(ci == NCH - 1)), pe)

        def emit_group_tail(g):
            gi_last = g * NCH + NCH - 1
            p.w("vector", pe, z_done[gi_last])
            zf_v = p.em("vector", lambda E: E.tensor_copy(zf_sb[:], pz[:]), vv)
            p.em("vector", lambda E: E.tensor_copy(
                dbuf[0:1, 0:512], po[0:1, 0:512]))
            l_v = p.em("vector", lambda E: E.tensor_tensor(
                l_bf[0:1, :], dbuf[0:1, 0:256], dbuf[0:1, 256:512], OP.add), vv)
            if g + 1 < NG:
                p.w("tensor", vv, zf_v)
                emit_qprime(g + 1)
            p.w("tensor", vv, l_v)
            p.em("tensor", lambda E: E.matmul(
                po[:, 300:301], l_bf[0:1, 0:P], ones_bf[0:1, 0:1],
                start=True, stop=True))
            t_lt = p.em("tensor", lambda E: E.matmul(
                po[:, 301:302], l_bf[0:1, P:2 * P], ones_bf[0:1, 0:1],
                start=True, stop=True), pe)
            p.w("vector", pe, t_lt)
            p.em("vector", lambda E: E.reciprocal(rl[:, 0:1], po[:, 300:301]))
            recip_v[g + 1] = p.em("vector", lambda E: E.reciprocal(
                rl[:, 1:2], po[:, 301:302]), vv)
            p.w("tensor", ss, bb_done[gi_last])
            for r in range(4):
                qs, eh = r // 2, r % 2
                bk_ = (r % 2) * 512
                if r >= 2:
                    p.w("tensor", vv, o_ev[(g, r - 2)])
                for dt in range(8):
                    p.em("tensor", lambda E, qs=qs, eh=eh, dt=dt, bk_=bk_:
                        E.matmul(
                            pqk[:, bk_:bk_ + 512],
                            zf_sb[:, dt * 256 + qs * P: dt * 256 + (qs + 1) * P],
                            G_sb[:, dt * KB + eh * 512: dt * KB + (eh + 1) * 512],
                            start=(dt == 0), stop=False))
                t_o = p.em("tensor", lambda E, qs=qs, eh=eh, bk_=bk_: E.matmul(
                    pqk[:, bk_:bk_ + 512],
                    l_bf[0:1, qs * P:(qs + 1) * P],
                    bpp_row[0:1, eh * 512:(eh + 1) * 512],
                    start=False, stop=True), pe)
                p.w("vector", pe, t_o)
                if r == 0:
                    p.w("vector", vv, recip_v[g + 1])
                    if g >= 1:
                        p.w("vector", d_x1w, 16 * g)
                o_ev[(g, r)] = p.em("vector", lambda E, qs=qs, eh=eh, bk_=bk_:
                    E.tensor_scalar(
                        x1b[:, qs * KB + eh * 512: qs * KB + (eh + 1) * 512],
                        pqk[:, bk_:bk_ + 512],
                        rl[:, qs:qs + 1], None, OP.mult), vv)
            if g == 0:
                p.w("gpsimd", pe, t_p0_done)
            else:
                p.w("gpsimd", vv, x1_last[g - 1])
            xr_d[g] = p.dma("gpsimd", d_wg,
                            xres[:].rearrange("p (t d) -> p t d", t=2),
                            xb_d[g * 256:(g + 1) * 256, :].rearrange(
                                "(t p) d -> p t d", p=P))
            x1_v = {}
            for qs in range(2):
                slot, v_rec = emit_stats(
                    lambda E, lo, n, qs=qs:
                        x1b[:, qs * KB + lo: qs * KB + lo + n], KB,
                    pre_wait=[("vector", vv, o_ev[(g, 2 * qs + 1)])])
                p.w("vector", vv, v_rec)
                v_x1t = p.em("vector", lambda E, qs=qs, slot=slot:
                    E.tensor_scalar(
                        x1t[:], x1b[:, qs * KB:(qs + 1) * KB],
                        mean_ap(slot), rstdc[:, slot:slot + 1],
                        OP.subtract, OP.mult), vv)
                mark_mean_read(slot, v_x1t)
                p.w("vector", vv, v_x1t)
                p.w("vector", d_wg, xr_d[g])
                x1_v[qs] = p.em("vector", lambda E, qs=qs: E.tensor_tensor(
                    x1b[:, qs * KB:(qs + 1) * KB], x1t[:],
                    xres[:, qs * KB:(qs + 1) * KB], OP.add), vv)
            x1_last[g] = x1_v[1]
            p.w("sync", vv, x1_v[1])
            p.dma("sync", d_x1w,
                  x1s_d[g * 256:(g + 1) * 256, :].rearrange(
                      "(t p) d -> p t d", p=P),
                  x1b[:].rearrange("p (t d) -> p t d", t=2))

        emit_qprime(0)
        for g in range(NG):
            emit_scores(g, 0)
            emit_scores(g, 1)
            emit_theta(g, 0)
            for ci in range(2, NCH):
                emit_scores(g, ci)
                emit_theta(g, ci - 1)
                emit_z(g, ci - 2)
            emit_theta(g, NCH - 1)
            emit_z(g, NCH - 2)
            emit_z(g, NCH - 1)
            emit_group_tail(g)
        t_p2_done = pe.n
        v_p2_done = vv.n

        if phases < 3:
            p.w("sync", d_x1w, 16 * NG)
            emit_dbg_taps(tapmap())
            return finish(nc, p)

        z_last = z_done[NG * NCH - 1]
        p.w("gpsimd", pe, z_last)
        p.w("scalar", pe, z_last)
        p.w("scalar", d_cast, cast_done)

        def w1_ap(sl):
            return (bass.AP(w1_sb, sl * 512, [[8 * HD, P], [HD, 8], [1, 512]]),
                    bass.AP(w1b_d, sl * 512, [[HD, P], [P * HD, 8], [1, 512]]))

        def w2_ap(sl):
            return (bass.AP(w2_sb, sl * 4 * D, [[NT * KB, P], [D, 4], [1, D]]),
                    bass.AP(w2b_d, sl * 4 * P * D, [[D, P], [P * D, 4], [1, D]]))

        d_w1m, d_w2m = {}, {}
        for sl in (0, 2, 4, 6):
            dst, src = w1_ap(sl)
            d_w1m[sl] = (d_wg, p.dma("gpsimd", d_wg, dst, src))
        for sl in (1, 3, 5, 7):
            dst, src = w1_ap(sl)
            d_w1m[sl] = (d_wsc, p.em(
                "scalar", lambda E, o=dst, i=src: E.dma_start(out=o, in_=i),
                d_wsc, 16))
        for sl in (5, 7):
            dst, src = w2_ap(sl)
            d_w2m[sl] = (d_wsc, p.em(
                "scalar", lambda E, o=dst, i=src: E.dma_start(out=o, in_=i),
                d_wsc, 16))
        p.w("gpsimd", d_x1w, 16 * NG)
        d_b2 = p.dma("gpsimd", d_wg, b2_row[0:1, :],
                     b2_d[:].rearrange("(a d) -> a d", a=1))
        for sl in (4, 6):
            dst, src = w2_ap(sl)
            d_w2m[sl] = (d_wg, p.dma("gpsimd", d_wg, dst, src))
        dx1_pre = {}
        p.w("sync", vv, v_p2_done)
        p.w("sync", d_x1w, 16 * NG)
        p.w("sync", d_cast, cast_done)
        for c3 in (0, 1):
            dx1_pre[c3] = p.dma(
                "sync", d_x1r,
                x1stg[c3 % 2][:].rearrange("p (t d) -> p t d", t=2),
                x1s_d[c3 * 256:(c3 + 1) * 256, :].rearrange(
                    "(t p) d -> p t d", p=P))
        for sl in (0, 1, 2, 3):
            dst, src = w2_ap(sl)
            d_w2m[sl] = (d_w2a, p.dma("sync", d_w2a, dst, src))

        gelu_done = {}
        y_out_v = {}
        y_mm_done = {}
        ln2_ev = {}
        ln2_tr = {}
        t_h31 = {}
        for c3 in range(NC3):
            par = c3 % 2
            if c3 < 2:
                dx1 = dx1_pre[c3]
            else:
                p.w("sync", vv, y_out_v[(c3 - 2, 1)])
                dx1 = p.dma("sync", d_x1r,
                            x1stg[par][:].rearrange("p (t d) -> p t d", t=2),
                            x1s_d[c3 * 256:(c3 + 1) * 256, :].rearrange(
                                "(t p) d -> p t d", p=P))
            for t in range(2):
                ti3 = c3 * 2 + t
                slot, v_rec = emit_stats(
                    lambda E, lo, n, par=par, t=t:
                        x1stg[par][:, t * KB + lo: t * KB + lo + n], KB,
                    pre_wait=[("vector", d_x1r, dx1)])
                p.w("vector", vv, v_rec)
                if ti3 >= 1:
                    p.w("vector", pe, ln2_tr[ti3 - 1])
                a_v = p.em("vector", lambda E, par=par, t=t, slot=slot:
                    E.tensor_scalar(
                        ln2tmp[:], x1stg[par][:, t * KB:(t + 1) * KB],
                        mean_ap(slot), rstdc[:, slot:slot + 1],
                        OP.subtract, OP.mult), vv)
                mark_mean_read(slot, a_v)
                p.w("tensor", vv, a_v)
                if ti3 == 0:
                    p.w("tensor", vv, v_p2_done)
                else:
                    p.w("tensor", vv, ln2_ev[ti3 - 1])
                for j in range(8):
                    fn = lambda E, j=j: E.transpose(
                        pbf[:, j * P:(j + 1) * P],
                        ln2tmp[:, j * P:(j + 1) * P], ident_bf[:])
                    t_tr = p.em("tensor", fn, pe) if j == 7 else p.em("tensor", fn)
                ln2_tr[ti3] = t_tr
                p.w("vector", pe, t_tr)
                if c3 >= 2 and t == 0:
                    p.w("vector", pe, t_h31[c3 - 2])
                ln2_ev[ti3] = p.em("vector", lambda E, par=par, t=t:
                    E.tensor_copy(
                        ln2_fm[par][:].rearrange("p (i k) -> p i k", i=8)
                        [:, :, t * P:(t + 1) * P],
                        pbf[:].rearrange("p (i t) -> p i t", i=8)), vv)
            p.w("tensor", vv, ln2_ev[c3 * 2 + 1])
            if c3 == 0:
                p.w("scalar", vv, v_b1)
            for jc in range(32):
                sl4 = (jc % 2) * 2 + (jc // 2) % 2
                if c3 == 0:
                    w1sem, w1cnt = d_w1m[jc // 4]
                    p.w("tensor", w1sem, w1cnt)
                prev = (c3 * 32 + jc) - 2
                if prev >= 0:
                    p.w("tensor", ss, gelu_done[prev])
                for ec in range(8):
                    fn = lambda E, jc=jc, ec=ec, sl4=sl4, par=par: E.matmul(
                        pqk[:, sl4 * 256:(sl4 + 1) * 256],
                        w1_sb[:, ec * HD + jc * P: ec * HD + (jc + 1) * P],
                        ln2_fm[par][:, ec * 256:(ec + 1) * 256],
                        start=(ec == 0), stop=(ec == 7))
                    t_h = p.em("tensor", fn, pe) if ec == 7 else p.em("tensor", fn)
                if jc == 31:
                    t_h31[c3] = t_h
                p.w("scalar", pe, t_h)
                if c3 >= 1 and jc == 0:
                    p.w("scalar", pe, y_mm_done[c3 - 1])
                gelu_done[c3 * 32 + jc] = p.em("scalar",
                    lambda E, jc=jc, sl4=sl4: E.activation(
                        h_sb[:, jc * 256:(jc + 1) * 256],
                        pqk[:, sl4 * 256:(sl4 + 1) * 256],
                        AF.Gelu, bias=b1_col[:, jc:jc + 1], scale=1.0), ss)
            p.w("tensor", ss, gelu_done[c3 * 32 + 31])
            if c3 == 0:
                p.w("tensor", d_wg, d_b2)
                p.w("tensor", vv, v_p2_done)
            else:
                p.w("tensor", vv, y_out_v[(c3 - 1, 1)])
            for st in range(2):
                for jc in range(32):
                    if c3 == 0 and st == 0 and jc % 4 == 0:
                        w2sem, w2cnt = d_w2m[jc // 4]
                        p.w("tensor", w2sem, w2cnt)
                    for eh in range(2):
                        p.em("tensor", lambda E, st=st, jc=jc, eh=eh: E.matmul(
                            pz[:, st * KB + eh * 512: st * KB + (eh + 1) * 512],
                            h_sb[:, jc * 256 + st * P: jc * 256 + (st + 1) * P],
                            w2_sb[:, jc * KB + eh * 512: jc * KB + (eh + 1) * 512],
                            start=(jc == 0), stop=False))
                for eh in range(2):
                    fn = lambda E, st=st, eh=eh: E.matmul(
                        pz[:, st * KB + eh * 512: st * KB + (eh + 1) * 512],
                        ones_bf[0:1, 0:P], b2_row[0:1, eh * 512:(eh + 1) * 512],
                        start=False, stop=True)
                    t_y = p.em("tensor", fn, pe) if eh == 1 else p.em("tensor", fn)
                if st == 1:
                    y_mm_done[c3] = t_y
                slot, v_rec = emit_stats(
                    lambda E, lo, n, st=st:
                        pz[:, st * KB + lo: st * KB + lo + n], KB,
                    pre_wait=[("vector", pe, t_y)])
                p.w("vector", vv, v_rec)
                v_mr = p.em("vector", lambda E, st=st, slot=slot: E.tensor_scalar(
                    mrstd[:, st:st + 1], mean_ap(slot), rstdc[:, slot:slot + 1],
                    None, OP.mult), vv)
                p.w("vector", vv, v_mr)
                if c3 >= 2 and st == 0:
                    p.w("vector", d_out, 16 * (c3 - 1))
                v_x1t = p.em("vector", lambda E, st=st, par=par: E.tensor_scalar(
                    x1t[:], x1stg[par][:, st * KB:(st + 1) * KB],
                    mrstd[:, st:st + 1], None, OP.subtract), vv)
                p.w("vector", vv, v_x1t)
                y_out_v[(c3, st)] = p.em("vector",
                    lambda E, st=st, par=par, slot=slot: E.scalar_tensor_tensor(
                        outstg[par][:, st * KB:(st + 1) * KB],
                        pz[:, st * KB:(st + 1) * KB],
                        rstdc[:, slot:slot + 1], x1t[:], OP.mult, OP.add), vv)
                mark_mean_read(slot, y_out_v[(c3, st)])
            p.w("sync", vv, y_out_v[(c3, 1)])
            p.dma("sync", d_out,
                  out_d[c3 * 256:(c3 + 1) * 256, :].rearrange(
                      "(t p) d -> p t d", p=P),
                  outstg[par][:].rearrange("p (t d) -> p t d", t=2))

        p.w("sync", d_out, 16 * NC3)
        emit_dbg_taps(tapmap())
        return finish(nc, p)


def finish(nc, p):
    with nc.Block() as block:
        block.sync(lambda E: [f(E) for f in p.ops["sync"]])
        block.gpsimd(lambda E: [f(E) for f in p.ops["gpsimd"]])
        block.vector(lambda E: [f(E) for f in p.ops["vector"]])
        block.scalar(lambda E: [f(E) for f in p.ops["scalar"]])
        block.tensor(lambda E: [f(E) for f in p.ops["tensor"]])
    return nc


_cache = {}


def kernel(L0_attn_x, L0_attn_Wq, L0_attn_bq, L0_attn_Wk, L0_attn_bk,
           L0_attn_Wv, L0_attn_bv, L0_attn_Wo, L0_attn_bo, L0_ffn_w1,
           L0_ffn_b1, L0_ffn_w2, L0_ffn_b2):
    x = np.ascontiguousarray(np.asarray(L0_attn_x, dtype=np.float32))
    ws = {k: np.ascontiguousarray(np.asarray(v, dtype=np.float32))
          for k, v in dict(
              Wq=L0_attn_Wq, Wk=L0_attn_Wk, Wv=L0_attn_Wv, Wo=L0_attn_Wo,
              bq=L0_attn_bq, bk=L0_attn_bk, bv=L0_attn_bv, bo=L0_attn_bo,
              w1=L0_ffn_w1, b1=L0_ffn_b1, w2=L0_ffn_w2, b2=L0_ffn_b2).items()}
    if "nc" not in _cache:
        _cache["nc"] = build_nc()
    nc = _cache["nc"]
    in_maps = []
    for c in range(8):
        b, h = c // 2, c % 2
        own = x[b, h * SQ:(h + 1) * SQ]
        other = x[b, (1 - h) * SQ:(2 - h) * SQ]
        m = {"xb": np.ascontiguousarray(np.concatenate([own, other], 0))}
        m.update(ws)
        in_maps.append(m)
    res = run_bass_kernel_spmd(nc, in_maps, core_ids=list(range(8)))
    out = np.empty((4, S, D), np.float32)
    for c, r in enumerate(res.results):
        b, h = c // 2, c % 2
        out[b, h * SQ:(h + 1) * SQ] = r["out"]
    return out


# version 06 (final)
# speedup vs baseline: 1.0174x; 1.0174x over previous
import numpy as np
from contextlib import ExitStack

import concourse.bass as bass
import concourse.mybir as mybir
from concourse.bass_utils import run_bass_kernel_spmd

F32 = mybir.dt.float32
BF16 = mybir.dt.bfloat16
AF = mybir.ActivationFunctionType
OP = mybir.AluOpType
AX = mybir.AxisListType

P = 128
D = 1024
S = 4096
SQ = 2048
HD = 4096
NT = 32
NG = 8
NCH = 16
NC3 = 8
EPS = 1e-5
CEXP = 0.6 - 20.0
KB = 1024


class Sem:
    def __init__(self, h):
        self.h = h
        self.n = 0


class Prog:
    ENGS = ("sync", "gpsimd", "vector", "scalar", "tensor")

    def __init__(self):
        self.ops = {e: [] for e in self.ENGS}

    def em(self, e, f, sem=None, by=1):
        if sem is None:
            self.ops[e].append(f)
            return None
        sem.n += by
        self.ops[e].append(lambda E, f=f, h=sem.h, b=by: f(E).then_inc(h, b))
        return sem.n

    def w(self, e, sem, val):
        if val is None or val <= 0:
            return
        self.ops[e].append(lambda E, h=sem.h, v=val: E.wait_ge(h, v))

    def dma(self, e, sem, out, in_):
        self.w(e, sem, sem.n)
        return self.em(e, lambda E, o=out, i=in_: E.dma_start(out=o, in_=i),
                       sem, 16)


def build_nc(dbg=(), phases=3, sub=""):
    nc = bass.Bass("TRN2", target_bir_lowering=False, debug=False)

    xb_d = nc.dram_tensor("xb", [S, D], F32, kind="ExternalInput")
    Wq_d = nc.dram_tensor("Wq", [D, D], F32, kind="ExternalInput")
    Wk_d = nc.dram_tensor("Wk", [D, D], F32, kind="ExternalInput")
    Wv_d = nc.dram_tensor("Wv", [D, D], F32, kind="ExternalInput")
    Wo_d = nc.dram_tensor("Wo", [D, D], F32, kind="ExternalInput")
    bq_d = nc.dram_tensor("bq", [D], F32, kind="ExternalInput")
    bk_d = nc.dram_tensor("bk", [D], F32, kind="ExternalInput")
    bv_d = nc.dram_tensor("bv", [D], F32, kind="ExternalInput")
    bo_d = nc.dram_tensor("bo", [D], F32, kind="ExternalInput")
    w1_d = nc.dram_tensor("w1", [D, HD], F32, kind="ExternalInput")
    b1_d = nc.dram_tensor("b1", [HD], F32, kind="ExternalInput")
    w2_d = nc.dram_tensor("w2", [HD, D], F32, kind="ExternalInput")
    b2_d = nc.dram_tensor("b2", [D], F32, kind="ExternalInput")
    out_d = nc.dram_tensor("out", [SQ, D], F32, kind="ExternalOutput")
    dbg_names = [n for n, _ in dbg]
    x1s_kind = "ExternalOutput" if "x1s" in dbg_names else "Internal"
    x1s_d = nc.dram_tensor("x1s", [SQ, D], F32, kind=x1s_kind)
    w1b_d = nc.dram_tensor("w1b", [D, HD], BF16, kind="Internal")
    w2b_d = nc.dram_tensor("w2b", [HD, D], BF16, kind="Internal")
    dbg_outs = {}
    for name, shape in dbg:
        if name == "x1s":
            continue
        dbg_outs[name] = nc.dram_tensor("dbg_" + name, shape, F32,
                                        kind="ExternalOutput")

    AR = 192512
    arena = nc.alloc_sbuf_tensor("arena", [P, AR], mybir.dt.uint8)
    base = nc.lookup_mloc(arena).addr

    def at(name, shape, dtype, off):
        return nc.alloc_sbuf_tensor_at(name, shape, dtype, offset=base + off)

    A, B, C, Dr, E, F = 0, 65536, 131072, 147456, 163840, 176128
    ln1_tm = at("ln1_tm", [P, NT * KB], BF16, A)
    ln1_fm = at("ln1_fm", [P, 8 * S], BF16, B)
    M_sb = at("M_sb", [P, 8 * KB], BF16, C)
    G_sb = at("G_sb", [P, 8 * KB], BF16, Dr)
    xstg = [at(f"xstg{i}", [P, KB], F32, E + i * 4096) for i in range(2)]
    qp_sb = at("qp_sb", [P, 8 * 256], BF16, E)
    zf_sb = at("zf_sb", [P, 8 * 256], BF16, E + 4096)
    ptb = at("ptb", [P, 2 * 512], BF16, E + 8192)
    bo_row = at("bo_row", [1, D], BF16, E + 8192)
    bias_st = at("bias_st", [32, P], F32, E + 10240)
    wot_sb = at("wot_sb", [P, 8 * KB], BF16, F)
    xres = at("xres", [P, 2 * KB], F32, F)
    x1b = at("x1b", [P, 2 * KB], F32, F + 8192)
    wA = at("wA", [P, 8 * KB], BF16, A + 32 * KB)
    wB = at("wB", [P, 8 * KB], BF16, A + 48 * KB)
    w1_sb = at("w1_sb", [P, 8 * HD], BF16, A)
    w2_sb = at("w2_sb", [P, NT * KB], BF16, B)
    x1stg = [at(f"x1stg{i}", [P, 2 * KB], F32, C + i * 8192) for i in range(2)]
    outstg = [at(f"outstg{i}", [P, 2 * KB], F32, Dr + i * 8192) for i in range(2)]
    h_sb = at("h_sb", [P, 32 * 256], BF16, E)
    ln2_fm = [at(f"ln2fm{i}", [P, 8 * 256], BF16, F + 4096 + i * 4096)
              for i in range(2)]
    ln2tmp = at("ln2tmp", [P, KB], BF16, F + 12288)
    b2_row = at("b2_row", [1, D], BF16, F + 14336)

    def sb(name, shape, dtype):
        return nc.alloc_sbuf_tensor(name, shape, dtype)

    ident_bf = sb("ident_bf", [P, P], BF16)
    ident_f = sb("ident_f", [32, 32], F32)
    ones_bf = sb("ones_bf", [1, P], BF16)
    ones_col = sb("ones_col", [P, 1], BF16)
    bqkv = sb("bqkv", [P, 24], BF16)
    wuv = sb("wuv", [P, 16], BF16)
    cq_f = sb("cq_f", [1, 1], F32)
    cq_row = sb("cq_row", [1, P], BF16)
    b1_col = sb("b1_col", [P, 32], F32)
    bpp_row = sb("bpp_row", [1, D], BF16)
    u_row = sb("u_row", [1, SQ], BF16)
    vcol = sb("vcol", [P, NT], F32)
    vch = sb("vch", [P, NT], F32)
    abuf = sb("abuf", [P, 512], F32)
    bbuf = sb("bbuf", [P, 512], F32)
    dbuf = sb("dbuf", [P, 512], F32)
    l_bf = sb("l_bf", [1, 256], BF16)
    rl = sb("rl", [P, 2], F32)
    consts = sb("consts", [P, 2], F32)
    x1t = sb("x1t", [P, KB], F32)
    bnst = sb("bnst", [P, 2 * 12], F32)
    bnagg = sb("bnagg", [P, 2 * 2], F32)
    sqv = sb("sqv", [P, 2], F32)
    rstdc = sb("rstdc", [P, 2], F32)
    mrstd = sb("mrstd", [P, 2], F32)

    pqk = nc.alloc_psum_tensor("pqk", [P, 1024], F32)
    pz = nc.alloc_psum_tensor("pz", [P, 2048], F32)
    po = nc.alloc_psum_tensor("po", [P, 512], F32)
    pbf = nc.alloc_psum_tensor("pbf", [P, 1024], BF16)

    p = Prog()

    with ExitStack() as ctx:
        sem = lambda n: Sem(ctx.enter_context(nc.semaphore(n)))
        d_wg = sem("d_wg")
        d_x = sem("d_x")
        d_x1w = sem("d_x1w")
        d_x1r = sem("d_x1r")
        d_out = sem("d_out")
        d_w2a = sem("d_w2a")
        d_wsc = sem("d_wsc")
        d_cast = sem("d_cast")
        d_dbg = sem("d_dbg")
        pe = sem("pe")
        vv = sem("vv")
        ss = sem("ss")
        gg = sem("gg")

        lnstate = {"i": 0, "sqrt": [0, 0], "recip": [0, 0],
                   "aggr": [0, 0], "mread": [0, 0]}

        def emit_stats(src_fn, width, pre_wait=None):
            i = lnstate["i"]
            lnstate["i"] += 1
            slot = i % 2
            for eng, sm, val in (pre_wait or []):
                p.w(eng, sm, val)
            p.w("vector", ss, lnstate["sqrt"][slot])
            p.w("vector", vv, lnstate["aggr"][slot])
            n5 = width // 512
            for j in range(n5):
                v_bn = p.em("vector", lambda E, j=j, slot=slot: E.bn_stats(
                    bnst[:, slot * 12 + j * 6: slot * 12 + (j + 1) * 6],
                    src_fn(E, j * 512, 512)), vv)
            p.w("vector", vv, v_bn)
            p.w("vector", vv, lnstate["mread"][slot])
            v_aggr = p.em("vector", lambda E, slot=slot, n5=n5: E.bn_aggr(
                bnagg[:, slot * 2: slot * 2 + 2],
                bnst[:, slot * 12: slot * 12 + 6 * n5]), vv)
            lnstate["aggr"][slot] = v_aggr
            p.w("scalar", vv, v_aggr)
            p.w("scalar", vv, lnstate["recip"][slot])
            s_sq = p.em("scalar", lambda E, slot=slot: E.activation(
                sqv[:, slot:slot + 1], bnagg[:, slot * 2 + 1: slot * 2 + 2],
                AF.Sqrt, bias=consts[:, 1:2], scale=1.0), ss)
            lnstate["sqrt"][slot] = s_sq
            p.w("vector", ss, s_sq)
            v_rec = p.em("vector", lambda E, slot=slot: E.reciprocal(
                rstdc[:, slot:slot + 1], sqv[:, slot:slot + 1]), vv)
            lnstate["recip"][slot] = v_rec
            return slot, v_rec

        def mean_ap(slot):
            return bnagg[:, slot * 2: slot * 2 + 1]

        def mark_mean_read(slot, count):
            lnstate["mread"][slot] = max(lnstate["mread"][slot], count)

        p.em("gpsimd", lambda E: E.memset(ones_bf[:], 1.0), gg)
        p.em("gpsimd", lambda E: E.memset(ones_col[:], 1.0), gg)
        p.em("gpsimd", lambda E: E.memset(consts[:, 0:1], CEXP), gg)
        p.em("gpsimd", lambda E: E.memset(consts[:, 1:2], EPS), gg)
        p.em("gpsimd", lambda E: E.memset(ident_bf[:], 0.0), gg)
        p.em("gpsimd", lambda E: E.memset(ident_f[:], 0.0), gg)
        p.w("gpsimd", gg, 6)
        p.em("gpsimd", lambda E: E.affine_select(
            out=ident_bf[:], in_=ident_bf[:], compare_op=OP.not_equal,
            fill=1.0, base=0, pattern=[[-1, P]], channel_multiplier=1), gg)
        g_setup = p.em("gpsimd", lambda E: E.affine_select(
            out=ident_f[:], in_=ident_f[:], compare_op=OP.not_equal,
            fill=1.0, base=0, pattern=[[-1, 32]], channel_multiplier=1), gg)

        wq_ap = Wq_d[:, :].rearrange("(c p) d -> p c d", p=P)
        wk_ap = Wk_d[:, :].rearrange("(c p) d -> p c d", p=P)
        wv_ap = Wv_d[:, :].rearrange("(c p) d -> p c d", p=P)
        wo_ap = Wo_d[:, :].rearrange("(c p) d -> p c d", p=P)
        wA3 = wA[:].rearrange("p (c d) -> p c d", c=8)
        wB3 = wB[:].rearrange("p (c d) -> p c d", c=8)
        d_wq = p.dma("gpsimd", d_wg, wA3, wq_ap)
        d_wk = p.dma("gpsimd", d_wg, wB3, wk_ap)
        d_bq = p.dma("gpsimd", d_wg, bias_st[0:8, :],
                     bq_d[:].rearrange("(c p) -> c p", p=P))
        d_bk = p.dma("gpsimd", d_wg, bias_st[8:16, :],
                     bk_d[:].rearrange("(c p) -> c p", p=P))
        d_bv = p.dma("gpsimd", d_wg, bias_st[16:24, :],
                     bv_d[:].rearrange("(c p) -> c p", p=P))
        d_bo = p.dma("gpsimd", d_wg, bo_row[0:1, :],
                     bo_d[:].rearrange("(a d) -> a d", a=1))

        p.w("scalar", gg, g_setup)
        p.w("tensor", gg, g_setup)
        p.w("tensor", d_wg, d_bv)
        t_bias = p.em("tensor", lambda E: E.transpose(
            po[:, 0:24], bias_st[0:24, :], ident_f[0:24, 0:24]), pe)
        p.w("vector", pe, t_bias)
        v_bqkv = p.em("vector", lambda E: E.tensor_copy(bqkv[:], po[:, 0:24]), vv)
        if sub == "bias":
            return finish(nc, p)

        p.w("tensor", d_wg, d_wk)
        m_evac = {}
        for dt in range(8):
            for nh in range(2):
                ti = dt * 2 + nh
                sl = ti % 4
                if ti >= 4:
                    p.w("tensor", vv, m_evac[ti - 4])
                for ec in range(8):
                    fn = lambda E, dt=dt, nh=nh, ec=ec, sl=sl: E.matmul(
                        pz[:, sl * 512:(sl + 1) * 512],
                        wA[:, ec * KB + dt * P: ec * KB + (dt + 1) * P],
                        wB[:, ec * KB + nh * 512: ec * KB + (nh + 1) * 512],
                        start=(ec == 0), stop=(ec == 7))
                    t_m = p.em("tensor", fn, pe) if ec == 7 else p.em("tensor", fn)
                p.w("vector", pe, t_m)
                m_evac[ti] = p.em("vector", lambda E, dt=dt, nh=nh, sl=sl:
                    E.tensor_copy(
                        M_sb[:, dt * KB + nh * 512: dt * KB + (nh + 1) * 512],
                        pz[:, sl * 512:(sl + 1) * 512]), vv)
        v_m_done = vv.n
        if sub == "m":
            return finish(nc, p)

        p.w("tensor", vv, v_bqkv)
        for dt in range(8):
            for ec in range(8):
                p.em("tensor", lambda E, dt=dt, ec=ec: E.matmul(
                    po[:, 64 + dt:65 + dt],
                    wA[:, ec * KB + dt * P: ec * KB + (dt + 1) * P],
                    bqkv[:, 8 + ec:9 + ec], start=(ec == 0), stop=(ec == 7)))
        for dt in range(8):
            for ec in range(8):
                p.em("tensor", lambda E, dt=dt, ec=ec: E.matmul(
                    po[:, 72 + dt:73 + dt],
                    wB[:, ec * KB + dt * P: ec * KB + (dt + 1) * P],
                    bqkv[:, ec:ec + 1], start=(ec == 0), stop=(ec == 7)))
        for ec in range(8):
            fn = lambda E, ec=ec: E.matmul(
                po[0:1, 80:81], bqkv[:, ec:ec + 1], bqkv[:, 8 + ec:9 + ec],
                start=(ec == 0), stop=(ec == 7))
            t_wuv = p.em("tensor", fn, pe) if ec == 7 else p.em("tensor", fn)
        p.w("vector", pe, t_wuv)
        p.em("vector", lambda E: E.tensor_copy(wuv[:], po[:, 64:80]), vv)
        v_cqf = p.em("vector", lambda E: E.tensor_copy(cq_f[0:1, :],
                                                       po[0:1, 80:81]), vv)
        p.w("vector", vv, v_cqf)
        v_cqr = p.em("vector", lambda E: E.tensor_scalar(
            cq_row[0:1, :], ones_bf[0:1, :], cq_f[0:1, 0:1], None, OP.mult), vv)
        if sub == "wuv":
            return finish(nc, p)

        p.w("gpsimd", pe, t_wuv)
        d_wv = p.dma("gpsimd", d_wg, wA3, wv_ap)
        d_wo = p.dma("gpsimd", d_wg, wB3, wo_ap)
        d_b1 = p.dma("gpsimd", d_wg, bias_st[0:32, :],
                     b1_d[:].rearrange("(c p) -> c p", p=P))
        p.dma("gpsimd", d_cast, w1b_d[:, :], w1_d[:, :])
        cast_done = p.dma("gpsimd", d_cast, w2b_d[:, :], w2_d[:, :])
        p.w("tensor", d_wg, d_b1)
        p.w("tensor", vv, v_bqkv)
        p.w("tensor", vv, v_cqf)
        t_b1 = p.em("tensor", lambda E: E.transpose(
            po[:, 0:32], bias_st[0:32, :], ident_f[0:32, 0:32]), pe)
        p.w("vector", pe, t_b1)
        v_b1 = p.em("vector", lambda E: E.tensor_copy(b1_col[:], po[:, 0:32]), vv)
        if sub == "b1":
            return finish(nc, p)

        p.w("tensor", d_wg, d_wo)
        wot_ev = {}
        for ec in range(8):
            if ec >= 1:
                p.w("tensor", vv, wot_ev[ec - 1])
            for j in range(8):
                fn = lambda E, ec=ec, j=j: E.transpose(
                    pbf[:, j * P:(j + 1) * P],
                    wB[:, j * KB + ec * P: j * KB + (ec + 1) * P],
                    ident_bf[:])
                t_wt = p.em("tensor", fn, pe) if j == 7 else p.em("tensor", fn)
            p.w("vector", pe, t_wt)
            wot_ev[ec] = p.em("vector", lambda E, ec=ec: E.tensor_copy(
                wot_sb[:, ec * KB:(ec + 1) * KB], pbf[:]), vv)
        v_wot_done = vv.n
        if sub == "wot":
            return finish(nc, p)

        p.w("tensor", d_wg, d_wv)
        p.w("tensor", vv, v_wot_done)
        g_evac = {}
        for dt in range(8):
            for nh in range(2):
                ti = dt * 2 + nh
                sl = ti % 4
                if ti >= 4:
                    p.w("tensor", vv, g_evac[ti - 4])
                for ec in range(8):
                    fn = lambda E, dt=dt, nh=nh, ec=ec, sl=sl: E.matmul(
                        pz[:, sl * 512:(sl + 1) * 512],
                        wA[:, ec * KB + dt * P: ec * KB + (dt + 1) * P],
                        wot_sb[:, ec * KB + nh * 512: ec * KB + (nh + 1) * 512],
                        start=(ec == 0), stop=(ec == 7))
                    t_g = p.em("tensor", fn, pe) if ec == 7 else p.em("tensor", fn)
                p.w("vector", pe, t_g)
                g_evac[ti] = p.em("vector", lambda E, dt=dt, nh=nh, sl=sl:
                    E.tensor_copy(
                        G_sb[:, dt * KB + nh * 512: dt * KB + (nh + 1) * 512],
                        pz[:, sl * 512:(sl + 1) * 512]), vv)

        p.w("tensor", d_wg, d_bo)
        v_bpp = v_b1
        for eh in range(2):
            p.w("tensor", vv, v_bpp)
            for ec in range(8):
                p.em("tensor", lambda E, eh=eh, ec=ec: E.matmul(
                    po[0:1, 0:512], bqkv[:, 16 + ec:17 + ec],
                    wot_sb[:, ec * KB + eh * 512: ec * KB + (eh + 1) * 512],
                    start=(ec == 0), stop=False))
            t_bpp = p.em("tensor", lambda E, eh=eh: E.matmul(
                po[0:1, 0:512], ones_bf[0:1, 0:1],
                bo_row[0:1, eh * 512:(eh + 1) * 512],
                start=False, stop=True), pe)
            p.w("vector", pe, t_bpp)
            v_bpp = p.em("vector", lambda E, eh=eh: E.tensor_copy(
                bpp_row[0:1, eh * 512:(eh + 1) * 512], po[0:1, 0:512]), vv)
        t_p0_done = pe.n
        v_p0_done = vv.n
        if sub == "p0a":
            return finish(nc, p)

        apply_v = {}
        fm_ev = {}
        for T in range(NT):
            par = T % 2
            if T >= 2:
                p.w("sync", vv, apply_v[T - 2])
            dx = p.dma("sync", d_x, xstg[par][:], xb_d[T * P:(T + 1) * P, :])
            slot, v_rec = emit_stats(
                lambda E, lo, n, par=par: xstg[par][:, lo:lo + n], KB,
                pre_wait=[("vector", d_x, dx)])
            p.w("vector", vv, v_rec)
            if T == 16:
                p.w("vector", pe, t_p0_done)
            apply_v[T] = p.em("vector", lambda E, T=T, par=par, slot=slot:
                E.tensor_scalar(
                    ln1_tm[:, T * KB:(T + 1) * KB], xstg[par][:],
                    mean_ap(slot), rstdc[:, slot:slot + 1],
                    OP.subtract, OP.mult), vv)
            mark_mean_read(slot, apply_v[T])
            p.w("tensor", vv, apply_v[T])
            if T >= 1:
                p.w("tensor", vv, fm_ev[T - 1])
            else:
                p.w("tensor", vv, v_p0_done)
            for j in range(8):
                fn = lambda E, T=T, j=j: E.transpose(
                    pbf[:, j * P:(j + 1) * P],
                    ln1_tm[:, T * KB + j * P: T * KB + (j + 1) * P],
                    ident_bf[:])
                t_tr = p.em("tensor", fn, pe) if j == 7 else p.em("tensor", fn)
            p.w("vector", pe, t_tr)
            fm_ev[T] = p.em("vector", lambda E, T=T: E.tensor_copy(
                ln1_fm[:].rearrange("p (i k) -> p i k", i=8)
                [:, :, T * P:(T + 1) * P],
                pbf[:].rearrange("p (i t) -> p i t", i=8)), vv)
        v_p1_done = vv.n
        if sub == "p1":
            return finish(nc, p)

        u_ev = {}
        for cu in range(8):
            p.w("tensor", vv, fm_ev[2 * cu + 1])
            p.w("tensor", vv, v_bpp if cu == 0 else u_ev[cu - 1])
            for dc in range(8):
                fn = lambda E, cu=cu, dc=dc: E.matmul(
                    po[0:1, 0:256], wuv[:, dc:dc + 1],
                    ln1_fm[:, dc * S + cu * 256: dc * S + (cu + 1) * 256],
                    start=(dc == 0), stop=(dc == 7))
                t_u = p.em("tensor", fn, pe) if dc == 7 else p.em("tensor", fn)
            p.w("vector", pe, t_u)
            u_ev[cu] = p.em("vector", lambda E, cu=cu: E.tensor_copy(
                u_row[0:1, cu * 256:(cu + 1) * 256], po[0:1, 0:256]), vv)
        p.w("tensor", vv, v_p1_done)
        p.w("tensor", vv, v_cqr)
        for T in range(NT):
            for dc in range(8):
                p.em("tensor", lambda E, T=T, dc=dc: E.matmul(
                    pqk[:, T:T + 1],
                    ln1_fm[:, dc * S + T * P: dc * S + (T + 1) * P],
                    wuv[:, 8 + dc:9 + dc], start=(dc == 0), stop=False))
            fn = lambda E, T=T: E.matmul(
                pqk[:, T:T + 1], cq_row[0:1, 0:P], ones_bf[0:1, 0:1],
                start=False, stop=True)
            t_vc = p.em("tensor", fn, pe) if T == NT - 1 else p.em("tensor", fn)
        p.w("vector", pe, t_vc)
        p.em("vector", lambda E: E.tensor_copy(vcol[:], pqk[:, 0:NT]))
        v_vcol = p.em("vector", lambda E: E.tensor_scalar(
            vch[:], pqk[:, 0:NT], 0.5, None, OP.mult), vv)

        def emit_dbg_taps(tapmap):
            for name, dout in dbg_outs.items():
                t, sm, val = tapmap[name]
                p.w("gpsimd", sm, val)
                p.em("gpsimd", lambda E, dout=dout, t=t:
                     E.dma_start(out=dout[:, :], in_=t[:]), d_dbg, 16)
            if dbg_outs:
                p.w("gpsimd", d_dbg, 16 * len(dbg_outs))

        def tapmap():
            return {
                "ln1_tm": (ln1_tm, vv, v_p1_done),
                "ln1_fm": (ln1_fm, vv, v_p1_done),
                "M": (M_sb, vv, v_m_done),
                "G": (G_sb, vv, v_p0_done),
                "wot": (wot_sb, vv, v_wot_done),
                "u": (u_row, vv, v_vcol),
                "vcol": (vcol, vv, v_vcol),
                "bpp": (bpp_row, vv, v_p0_done),
                "b1col": (b1_col, vv, v_p0_done),
                "qp": (qp_sb, vv, vv.n),
                "zf": (zf_sb, vv, vv.n),
                "x1b": (x1b, vv, vv.n),
                "lbf": (l_bf, vv, vv.n),
                "rl": (rl, vv, vv.n),
                "w1sb": (w1_sb, vv, vv.n),
                "w2sb": (w2_sb, vv, vv.n),
                "hsb": (h_sb, vv, vv.n),
                "b2row": (b2_row, vv, vv.n),
            }

        if phases < 2:
            emit_dbg_taps(tapmap())
            return finish(nc, p)

        s_done = {}
        relu_done = {}
        bb_done = {}
        d1_done = {}
        e_done = {}
        exp_done = {}
        z_done = {}
        qp_ev = {}
        o_ev = {}
        recip_v = {0: v_vcol}
        xr_d = {}
        x1_last = {}

        def emit_qprime(g):
            if g == 0:
                p.w("tensor", vv, v_p1_done)
                p.w("tensor", vv, v_m_done)
            for dt in range(8):
                for ec in range(8):
                    fn = lambda E, g=g, dt=dt, ec=ec: E.matmul(
                        pz[:, dt * 256:(dt + 1) * 256],
                        M_sb[:, ec * KB + dt * P: ec * KB + (dt + 1) * P],
                        ln1_fm[:, ec * S + g * 256: ec * S + (g + 1) * 256],
                        start=(ec == 0), stop=(ec == 7))
                    last = (dt == 7 and ec == 7)
                    t_qp = p.em("tensor", fn, pe) if last else p.em("tensor", fn)
            p.w("vector", pe, t_qp)
            qp_ev[g] = p.em("vector", lambda E: E.tensor_copy(qp_sb[:], pz[:]), vv)

        def emit_scores(g, ci):
            par = ci % 2
            gi = g * NCH + ci
            if gi >= 2:
                p.w("tensor", vv, relu_done[gi - 2])
                p.w("tensor", ss, bb_done[gi - 2])
            if ci <= 1 and g >= 1:
                p.w("tensor", vv, o_ev[(g - 1, 2 + ci)])
            if ci == 0:
                p.w("tensor", vv, qp_ev[g])
            for ki in range(2):
                kc = 2 * ci + ki
                for dc in range(8):
                    p.em("tensor", lambda E, par=par, ki=ki, kc=kc, dc=dc:
                        E.matmul(
                            pqk[:, par * 512 + ki * 256:
                                par * 512 + (ki + 1) * 256],
                            ln1_fm[:, dc * S + kc * P: dc * S + (kc + 1) * P],
                            qp_sb[:, dc * 256:(dc + 1) * 256],
                            start=(dc == 0), stop=False))
                fn = lambda E, par=par, ki=ki, g=g: E.matmul(
                    pqk[:, par * 512 + ki * 256: par * 512 + (ki + 1) * 256],
                    ones_bf[0:1, 0:P], u_row[0:1, g * 256:(g + 1) * 256],
                    start=False, stop=True)
                t_s = p.em("tensor", fn, pe) if ki == 1 else p.em("tensor", fn)
            s_done[gi] = t_s

        def emit_theta(g, ci):
            par = ci % 2
            gi = g * NCH + ci
            p.w("scalar", pe, s_done[gi])
            if gi >= 1:
                p.w("scalar", vv, e_done[gi - 1])
                p.w("scalar", ss, exp_done[gi - 1])
            for ki in range(2):
                kc = 2 * ci + ki
                p.em("scalar", lambda E, par=par, ki=ki, kc=kc: E.activation(
                    abuf[:, ki * 256:(ki + 1) * 256],
                    pqk[:, par * 512 + ki * 256: par * 512 + (ki + 1) * 256],
                    AF.Tanh, bias=vch[:, kc:kc + 1], scale=0.5))
            for ki in range(2):
                kc = 2 * ci + ki
                fn = lambda E, par=par, ki=ki, kc=kc: E.activation(
                    bbuf[:, ki * 256:(ki + 1) * 256],
                    pqk[:, par * 512 + ki * 256: par * 512 + (ki + 1) * 256],
                    AF.Tanh, bias=vcol[:, kc:kc + 1], scale=1.0)
                t_bb = p.em("scalar", fn, ss) if ki == 1 else p.em("scalar", fn)
            bb_done[gi] = t_bb
            p.w("vector", ss, bb_done[gi])
            d1_done[gi] = p.em("vector", lambda E: E.scalar_tensor_tensor(
                dbuf[:], bbuf[:], 1.5, abuf[:], OP.mult, OP.add), vv)
            p.w("vector", vv, d1_done[gi])
            for ki in range(2):
                kc = 2 * ci + ki
                fn = lambda E, par=par, ki=ki, kc=kc: E.tensor_scalar(
                    abuf[:, ki * 256:(ki + 1) * 256],
                    pqk[:, par * 512 + ki * 256: par * 512 + (ki + 1) * 256],
                    vcol[:, kc:kc + 1], 0.0, OP.add, OP.max)
                t_rl = p.em("vector", fn, vv) if ki == 1 else p.em("vector", fn)
            relu_done[gi] = t_rl
            p.w("vector", vv, relu_done[gi])
            if gi >= 1:
                p.w("vector", ss, exp_done[gi - 1])
            e_done[gi] = p.em("vector", lambda E: E.tensor_tensor(
                bbuf[:], abuf[:], dbuf[:], OP.add), vv)
            p.w("scalar", vv, e_done[gi])
            if gi >= 2:
                p.w("scalar", pe, z_done[gi - 2])
            exp_done[gi] = p.em("scalar", lambda E, par=par: E.activation(
                ptb[:, par * 512:(par + 1) * 512], bbuf[:], AF.Exp,
                bias=consts[:, 0:1], scale=0.1), ss)

        def emit_z(g, ci):
            par = ci % 2
            gi = g * NCH + ci
            p.w("tensor", ss, exp_done[gi])
            if ci == 0 and g >= 1:
                p.w("tensor", vv, o_ev[(g - 1, 3)])
            for ki in range(2):
                kc = 2 * ci + ki
                for dt in range(8):
                    p.em("tensor", lambda E, par=par, ki=ki, kc=kc, dt=dt:
                        E.matmul(
                            pz[:, dt * 256:(dt + 1) * 256],
                            ln1_tm[:, kc * KB + dt * P: kc * KB + (dt + 1) * P],
                            ptb[:, par * 512 + ki * 256:
                                par * 512 + (ki + 1) * 256],
                            start=(ci == 0 and ki == 0 and dt % 2 == 0),
                            stop=(ci == NCH - 1 and ki == 1 and dt % 2 == 1)))
            z_done[gi] = p.em("tensor", lambda E, par=par, ci=ci: E.matmul(
                po[0:1, 0:512], ones_col[:, 0:1],
                ptb[:, par * 512: (par + 1) * 512],
                start=(ci == 0), stop=(ci == NCH - 1)), pe)

        def emit_group_tail(g):
            gi_last = g * NCH + NCH - 1
            p.w("vector", pe, z_done[gi_last])
            zf_v = p.em("vector", lambda E: E.tensor_copy(zf_sb[:], pz[:]), vv)
            p.w("vector", vv, e_done[gi_last])
            cp_v = p.em("vector", lambda E: E.tensor_copy(
                dbuf[0:1, 0:512], po[0:1, 0:512]), vv)
            p.w("vector", vv, cp_v)
            l_v = p.em("vector", lambda E: E.tensor_tensor(
                l_bf[0:1, :], dbuf[0:1, 0:256], dbuf[0:1, 256:512], OP.add), vv)
            if g + 1 < NG:
                p.w("tensor", vv, zf_v)
                emit_qprime(g + 1)
            p.w("tensor", vv, l_v)
            p.em("tensor", lambda E: E.matmul(
                po[:, 300:301], l_bf[0:1, 0:P], ones_bf[0:1, 0:1],
                start=True, stop=True))
            t_lt = p.em("tensor", lambda E: E.matmul(
                po[:, 301:302], l_bf[0:1, P:2 * P], ones_bf[0:1, 0:1],
                start=True, stop=True), pe)
            p.w("vector", pe, t_lt)
            p.em("vector", lambda E: E.reciprocal(rl[:, 0:1], po[:, 300:301]))
            recip_v[g + 1] = p.em("vector", lambda E: E.reciprocal(
                rl[:, 1:2], po[:, 301:302]), vv)
            p.w("tensor", ss, bb_done[gi_last])
            for r in range(4):
                qs, eh = r // 2, r % 2
                bk_ = (r % 2) * 512
                if r >= 2:
                    p.w("tensor", vv, o_ev[(g, r - 2)])
                for dt in range(8):
                    p.em("tensor", lambda E, qs=qs, eh=eh, dt=dt, bk_=bk_:
                        E.matmul(
                            pqk[:, bk_:bk_ + 512],
                            zf_sb[:, dt * 256 + qs * P: dt * 256 + (qs + 1) * P],
                            G_sb[:, dt * KB + eh * 512: dt * KB + (eh + 1) * 512],
                            start=(dt == 0), stop=False))
                t_o = p.em("tensor", lambda E, qs=qs, eh=eh, bk_=bk_: E.matmul(
                    pqk[:, bk_:bk_ + 512],
                    l_bf[0:1, qs * P:(qs + 1) * P],
                    bpp_row[0:1, eh * 512:(eh + 1) * 512],
                    start=False, stop=True), pe)
                p.w("vector", pe, t_o)
                if r == 0:
                    p.w("vector", vv, recip_v[g + 1])
                    if g >= 1:
                        p.w("vector", d_x1w, 16 * g)
                o_ev[(g, r)] = p.em("vector", lambda E, qs=qs, eh=eh, bk_=bk_:
                    E.tensor_scalar(
                        x1b[:, qs * KB + eh * 512: qs * KB + (eh + 1) * 512],
                        pqk[:, bk_:bk_ + 512],
                        rl[:, qs:qs + 1], None, OP.mult), vv)
            if g == 0:
                p.w("gpsimd", pe, t_p0_done)
            else:
                p.w("gpsimd", vv, x1_last[g - 1])
            xr_d[g] = p.dma("gpsimd", d_wg,
                            xres[:].rearrange("p (t d) -> p t d", t=2),
                            xb_d[g * 256:(g + 1) * 256, :].rearrange(
                                "(t p) d -> p t d", p=P))
            x1_v = {}
            for qs in range(2):
                slot, v_rec = emit_stats(
                    lambda E, lo, n, qs=qs:
                        x1b[:, qs * KB + lo: qs * KB + lo + n], KB,
                    pre_wait=[("vector", vv, o_ev[(g, 2 * qs + 1)])])
                p.w("vector", vv, v_rec)
                v_x1t = p.em("vector", lambda E, qs=qs, slot=slot:
                    E.tensor_scalar(
                        x1t[:], x1b[:, qs * KB:(qs + 1) * KB],
                        mean_ap(slot), rstdc[:, slot:slot + 1],
                        OP.subtract, OP.mult), vv)
                mark_mean_read(slot, v_x1t)
                p.w("vector", vv, v_x1t)
                p.w("vector", d_wg, xr_d[g])
                x1_v[qs] = p.em("vector", lambda E, qs=qs: E.tensor_tensor(
                    x1b[:, qs * KB:(qs + 1) * KB], x1t[:],
                    xres[:, qs * KB:(qs + 1) * KB], OP.add), vv)
            x1_last[g] = x1_v[1]
            p.w("sync", vv, x1_v[1])
            p.dma("sync", d_x1w,
                  x1s_d[g * 256:(g + 1) * 256, :].rearrange(
                      "(t p) d -> p t d", p=P),
                  x1b[:].rearrange("p (t d) -> p t d", t=2))

        emit_qprime(0)
        for g in range(NG):
            emit_scores(g, 0)
            emit_scores(g, 1)
            emit_theta(g, 0)
            for ci in range(2, NCH):
                emit_scores(g, ci)
                emit_theta(g, ci - 1)
                emit_z(g, ci - 2)
            emit_theta(g, NCH - 1)
            emit_z(g, NCH - 2)
            emit_z(g, NCH - 1)
            emit_group_tail(g)
        t_p2_done = pe.n
        v_p2_done = vv.n

        if phases < 3:
            p.w("sync", d_x1w, 16 * NG)
            emit_dbg_taps(tapmap())
            return finish(nc, p)

        z_last = z_done[NG * NCH - 1]
        p.w("gpsimd", pe, z_last)
        p.w("scalar", pe, z_last)
        p.w("scalar", d_cast, cast_done)

        def w1_ap(sl):
            return (bass.AP(w1_sb, sl * 512, [[8 * HD, P], [HD, 8], [1, 512]]),
                    bass.AP(w1b_d, sl * 512, [[HD, P], [P * HD, 8], [1, 512]]))

        def w2_ap(sl):
            return (bass.AP(w2_sb, sl * 4 * D, [[NT * KB, P], [D, 4], [1, D]]),
                    bass.AP(w2b_d, sl * 4 * P * D, [[D, P], [P * D, 4], [1, D]]))

        d_w1m, d_w2m = {}, {}
        for sl in (0, 2, 4, 6):
            dst, src = w1_ap(sl)
            d_w1m[sl] = (d_wg, p.dma("gpsimd", d_wg, dst, src))
        for sl in (1, 3, 5, 7):
            dst, src = w1_ap(sl)
            d_w1m[sl] = (d_wsc, p.em(
                "scalar", lambda E, o=dst, i=src: E.dma_start(out=o, in_=i),
                d_wsc, 16))
        for sl in (5, 7):
            dst, src = w2_ap(sl)
            d_w2m[sl] = (d_wsc, p.em(
                "scalar", lambda E, o=dst, i=src: E.dma_start(out=o, in_=i),
                d_wsc, 16))
        p.w("gpsimd", d_x1w, 16 * NG)
        d_b2 = p.dma("gpsimd", d_wg, b2_row[0:1, :],
                     b2_d[:].rearrange("(a d) -> a d", a=1))
        for sl in (4, 6):
            dst, src = w2_ap(sl)
            d_w2m[sl] = (d_wg, p.dma("gpsimd", d_wg, dst, src))
        dx1_pre = {}
        p.w("sync", vv, v_p2_done)
        p.w("sync", d_x1w, 16 * NG)
        p.w("sync", d_cast, cast_done)
        for c3 in (0, 1):
            dx1_pre[c3] = p.dma(
                "sync", d_x1r,
                x1stg[c3 % 2][:].rearrange("p (t d) -> p t d", t=2),
                x1s_d[c3 * 256:(c3 + 1) * 256, :].rearrange(
                    "(t p) d -> p t d", p=P))
        for sl in (0, 1, 2, 3):
            dst, src = w2_ap(sl)
            d_w2m[sl] = (d_w2a, p.dma("sync", d_w2a, dst, src))

        gelu_done = {}
        y_out_v = {}
        y_mm_done = {}
        ln2_ev = {}
        ln2_tr = {}
        t_h31 = {}
        for c3 in range(NC3):
            par = c3 % 2
            if c3 < 2:
                dx1 = dx1_pre[c3]
            else:
                p.w("sync", vv, y_out_v[(c3 - 2, 1)])
                dx1 = p.dma("sync", d_x1r,
                            x1stg[par][:].rearrange("p (t d) -> p t d", t=2),
                            x1s_d[c3 * 256:(c3 + 1) * 256, :].rearrange(
                                "(t p) d -> p t d", p=P))
            for t in range(2):
                ti3 = c3 * 2 + t
                slot, v_rec = emit_stats(
                    lambda E, lo, n, par=par, t=t:
                        x1stg[par][:, t * KB + lo: t * KB + lo + n], KB,
                    pre_wait=[("vector", d_x1r, dx1)])
                p.w("vector", vv, v_rec)
                if ti3 >= 1:
                    p.w("vector", pe, ln2_tr[ti3 - 1])
                a_v = p.em("vector", lambda E, par=par, t=t, slot=slot:
                    E.tensor_scalar(
                        ln2tmp[:], x1stg[par][:, t * KB:(t + 1) * KB],
                        mean_ap(slot), rstdc[:, slot:slot + 1],
                        OP.subtract, OP.mult), vv)
                mark_mean_read(slot, a_v)
                p.w("tensor", vv, a_v)
                if ti3 == 0:
                    p.w("tensor", vv, v_p2_done)
                else:
                    p.w("tensor", vv, ln2_ev[ti3 - 1])
                for j in range(8):
                    fn = lambda E, j=j: E.transpose(
                        pbf[:, j * P:(j + 1) * P],
                        ln2tmp[:, j * P:(j + 1) * P], ident_bf[:])
                    t_tr = p.em("tensor", fn, pe) if j == 7 else p.em("tensor", fn)
                ln2_tr[ti3] = t_tr
                p.w("vector", pe, t_tr)
                if c3 >= 2 and t == 0:
                    p.w("vector", pe, t_h31[c3 - 2])
                ln2_ev[ti3] = p.em("vector", lambda E, par=par, t=t:
                    E.tensor_copy(
                        ln2_fm[par][:].rearrange("p (i k) -> p i k", i=8)
                        [:, :, t * P:(t + 1) * P],
                        pbf[:].rearrange("p (i t) -> p i t", i=8)), vv)
            p.w("tensor", vv, ln2_ev[c3 * 2 + 1])
            if c3 == 0:
                p.w("scalar", vv, v_b1)
            for jc in range(32):
                sl4 = (jc % 2) * 2 + (jc // 2) % 2
                if c3 == 0:
                    w1sem, w1cnt = d_w1m[jc // 4]
                    p.w("tensor", w1sem, w1cnt)
                prev = (c3 * 32 + jc) - 2
                if prev >= 0:
                    p.w("tensor", ss, gelu_done[prev])
                for ec in range(8):
                    fn = lambda E, jc=jc, ec=ec, sl4=sl4, par=par: E.matmul(
                        pqk[:, sl4 * 256:(sl4 + 1) * 256],
                        w1_sb[:, ec * HD + jc * P: ec * HD + (jc + 1) * P],
                        ln2_fm[par][:, ec * 256:(ec + 1) * 256],
                        start=(ec == 0), stop=(ec == 7))
                    t_h = p.em("tensor", fn, pe) if ec == 7 else p.em("tensor", fn)
                if jc == 31:
                    t_h31[c3] = t_h
                p.w("scalar", pe, t_h)
                if c3 >= 1 and jc == 0:
                    p.w("scalar", pe, y_mm_done[c3 - 1])
                gelu_done[c3 * 32 + jc] = p.em("scalar",
                    lambda E, jc=jc, sl4=sl4: E.activation(
                        h_sb[:, jc * 256:(jc + 1) * 256],
                        pqk[:, sl4 * 256:(sl4 + 1) * 256],
                        AF.Gelu, bias=b1_col[:, jc:jc + 1], scale=1.0), ss)
            p.w("tensor", ss, gelu_done[c3 * 32 + 31])
            if c3 == 0:
                p.w("tensor", d_wg, d_b2)
                p.w("tensor", vv, v_p2_done)
            else:
                p.w("tensor", vv, y_out_v[(c3 - 1, 1)])
            for st in range(2):
                for jc in range(32):
                    if c3 == 0 and st == 0 and jc % 4 == 0:
                        w2sem, w2cnt = d_w2m[jc // 4]
                        p.w("tensor", w2sem, w2cnt)
                    for eh in range(2):
                        p.em("tensor", lambda E, st=st, jc=jc, eh=eh: E.matmul(
                            pz[:, st * KB + eh * 512: st * KB + (eh + 1) * 512],
                            h_sb[:, jc * 256 + st * P: jc * 256 + (st + 1) * P],
                            w2_sb[:, jc * KB + eh * 512: jc * KB + (eh + 1) * 512],
                            start=(jc == 0), stop=False))
                for eh in range(2):
                    fn = lambda E, st=st, eh=eh: E.matmul(
                        pz[:, st * KB + eh * 512: st * KB + (eh + 1) * 512],
                        ones_bf[0:1, 0:P], b2_row[0:1, eh * 512:(eh + 1) * 512],
                        start=False, stop=True)
                    t_y = p.em("tensor", fn, pe) if eh == 1 else p.em("tensor", fn)
                if st == 1:
                    y_mm_done[c3] = t_y
                slot, v_rec = emit_stats(
                    lambda E, lo, n, st=st:
                        pz[:, st * KB + lo: st * KB + lo + n], KB,
                    pre_wait=[("vector", pe, t_y)])
                p.w("vector", vv, v_rec)
                v_mr = p.em("vector", lambda E, st=st, slot=slot: E.tensor_scalar(
                    mrstd[:, st:st + 1], mean_ap(slot), rstdc[:, slot:slot + 1],
                    None, OP.mult), vv)
                p.w("vector", vv, v_mr)
                if c3 >= 2 and st == 0:
                    p.w("vector", d_out, 16 * (c3 - 1))
                v_x1t = p.em("vector", lambda E, st=st, par=par: E.tensor_scalar(
                    x1t[:], x1stg[par][:, st * KB:(st + 1) * KB],
                    mrstd[:, st:st + 1], None, OP.subtract), vv)
                p.w("vector", vv, v_x1t)
                y_out_v[(c3, st)] = p.em("vector",
                    lambda E, st=st, par=par, slot=slot: E.scalar_tensor_tensor(
                        outstg[par][:, st * KB:(st + 1) * KB],
                        pz[:, st * KB:(st + 1) * KB],
                        rstdc[:, slot:slot + 1], x1t[:], OP.mult, OP.add), vv)
                mark_mean_read(slot, y_out_v[(c3, st)])
            p.w("sync", vv, y_out_v[(c3, 1)])
            p.dma("sync", d_out,
                  out_d[c3 * 256:(c3 + 1) * 256, :].rearrange(
                      "(t p) d -> p t d", p=P),
                  outstg[par][:].rearrange("p (t d) -> p t d", t=2))

        p.w("sync", d_out, 16 * NC3)
        emit_dbg_taps(tapmap())
        return finish(nc, p)


def finish(nc, p):
    with nc.Block() as block:
        block.sync(lambda E: [f(E) for f in p.ops["sync"]])
        block.gpsimd(lambda E: [f(E) for f in p.ops["gpsimd"]])
        block.vector(lambda E: [f(E) for f in p.ops["vector"]])
        block.scalar(lambda E: [f(E) for f in p.ops["scalar"]])
        block.tensor(lambda E: [f(E) for f in p.ops["tensor"]])
    return nc


_cache = {}


def kernel(L0_attn_x, L0_attn_Wq, L0_attn_bq, L0_attn_Wk, L0_attn_bk,
           L0_attn_Wv, L0_attn_bv, L0_attn_Wo, L0_attn_bo, L0_ffn_w1,
           L0_ffn_b1, L0_ffn_w2, L0_ffn_b2):
    x = np.ascontiguousarray(np.asarray(L0_attn_x, dtype=np.float32))
    ws = {k: np.ascontiguousarray(np.asarray(v, dtype=np.float32))
          for k, v in dict(
              Wq=L0_attn_Wq, Wk=L0_attn_Wk, Wv=L0_attn_Wv, Wo=L0_attn_Wo,
              bq=L0_attn_bq, bk=L0_attn_bk, bv=L0_attn_bv, bo=L0_attn_bo,
              w1=L0_ffn_w1, b1=L0_ffn_b1, w2=L0_ffn_w2, b2=L0_ffn_b2).items()}
    if "nc" not in _cache:
        _cache["nc"] = build_nc()
    nc = _cache["nc"]
    in_maps = []
    for c in range(8):
        b, h = c // 2, c % 2
        own = x[b, h * SQ:(h + 1) * SQ]
        other = x[b, (1 - h) * SQ:(2 - h) * SQ]
        m = {"xb": np.ascontiguousarray(np.concatenate([own, other], 0))}
        m.update(ws)
        in_maps.append(m)
    res = run_bass_kernel_spmd(nc, in_maps, core_ids=list(range(8)))
    out = np.empty((4, S, D), np.float32)
    for c, r in enumerate(res.results):
        b, h = c // 2, c % 2
        out[b, h * SQ:(h + 1) * SQ] = r["out"]
    return out
